# version 40
# baseline (speedup 1.0000x reference)
import numpy as np
import concourse.bass as bass
import concourse.mybir as mybir
from concourse.bass_utils import run_bass_kernel_spmd

F32 = mybir.dt.float32
BF16 = mybir.dt.bfloat16
AF = mybir.ActivationFunctionType
OP = mybir.AluOpType
AX = mybir.AxisListType

C23 = 12582912.0
C52 = 6755399441055744.0
LN_EPS = 1e-5
D = 1024
NDT = D // 128


class Planner:

    ENGINES = ("sync", "gpsimd", "tensor", "vector", "scalar")
    NDMA = 8

    def __init__(self):
        self.steps = {e: [] for e in self.ENGINES}
        self.count = {e: 0 for e in self.ENGINES}
        self.dma_idx = {e: 0 for e in self.ENGINES}
        self.water = {e: {} for e in self.ENGINES}
        self.writer = {}
        self.readers = {}

    def _need(self, eng, tag, needs):
        if tag is None:
            return
        sem, val = tag
        if self.water[eng].get(sem, 0) >= val:
            return
        if needs.get(sem, 0) < val:
            needs[sem] = val

    def step(self, eng, fn, reads=(), writes=(), dma=False):
        needs = {}
        for k in reads:
            self._need(eng, self.writer.get(k), needs)
        for k in writes:
            self._need(eng, self.writer.get(k), needs)
            for t in self.readers.get(k, ()):
                self._need(eng, t, needs)
        if dma:
            i = self.dma_idx[eng]
            r, uses = i % self.NDMA, i // self.NDMA
            dsem = f"{eng}_dma{r}"
            if uses > 0:
                self._need(eng, (dsem, 16 * uses), needs)
            self.dma_idx[eng] = i + 1
            tag = (dsem, 16 * (uses + 1))
        else:
            self.count[eng] += 1
            tag = (eng, self.count[eng])
        waits = sorted(needs.items())
        for sem, val in waits:
            self.water[eng][sem] = val
        for k in reads:
            self.readers.setdefault(k, []).append(tag)
        for k in writes:
            self.writer[k] = tag
            self.readers[k] = []
        self.steps[eng].append((waits, fn, dma, tag))
        return tag

    def step_multi(self, eng, items, dma=False):
        assert not dma
        needs = {}
        for fn, reads, writes in items:
            for k in reads:
                self._need(eng, self.writer.get(k), needs)
            for k in writes:
                self._need(eng, self.writer.get(k), needs)
                for t in self.readers.get(k, ()):
                    self._need(eng, t, needs)
        self.count[eng] += 1
        tag = (eng, self.count[eng])
        waits = sorted(needs.items())
        for sem, val in waits:
            self.water[eng][sem] = val
        for fn, reads, writes in items:
            for k in reads:
                self.readers.setdefault(k, []).append(tag)
            for k in writes:
                self.writer[k] = tag
                self.readers[k] = []
        self.steps[eng].append((waits, [f for f, _, _ in items], False, tag))
        return tag

    def alias(self, dst_keys, src_keys):
        tags = []
        for k in src_keys:
            if k in self.writer and self.writer[k] is not None:
                tags.append(self.writer[k])
            tags.extend(self.readers.get(k, ()))
        for k in dst_keys:
            self.writer[k] = None
            self.readers[k] = list(tags)

    def replay(self, eng_name, e, sems):
        for waits, fn, dma, tag in self.steps[eng_name]:
            for sem_name, val in waits:
                e.wait_ge(sems[sem_name], val)
            if isinstance(fn, list):
                for f in fn[:-1]:
                    f(e)
                ins = fn[-1](e)
            else:
                ins = fn(e)
            ins.then_inc(sems[tag[0]], 16 if dma else 1)


def build_nc(T=8192, CT=256):
    NCH = T // CT
    NTT = CT // 128
    nc = bass.Bass("TRN2", target_bir_lowering=False)

    xd = nc.dram_tensor("x", [T, D], F32, kind="ExternalInput")
    wbTd = nc.dram_tensor("wbT", [D, D], F32, kind="ExternalInput")
    wcTd = nc.dram_tensor("wcT", [D, D], F32, kind="ExternalInput")
    rhsd = nc.dram_tensor("rhs", [D, D], F32, kind="ExternalInput")
    lrd = nc.dram_tensor("lraw", [D], F32, kind="ExternalInput")
    bbd = nc.dram_tensor("bb", [D], F32, kind="ExternalInput")
    bcd = nc.dram_tensor("bc", [D], F32, kind="ExternalInput")
    gad = nc.dram_tensor("gamma", [D], F32, kind="ExternalInput")
    bed = nc.dram_tensor("beta", [D], F32, kind="ExternalInput")
    yd = nc.dram_tensor("y", [T, D], F32, kind="ExternalOutput")
    csrow_d = nc.dram_tensor("csrow", [D], F32)

    import contextlib
    with contextlib.ExitStack() as stk:
        E = stk.enter_context

        def sbt(name, shape, dt=F32):
            return E(nc.sbuf_tensor(name, shape, dt))

        wbh = sbt("wbh", [128, NDT * D], BF16)
        wbl = sbt("wbl", [128, NDT * D], BF16)
        wch = sbt("wch", [128, NDT * D], BF16)
        wcl = sbt("wcl", [128, NDT * D], BF16)
        qrs = sbt("qrs", [128, NDT * D], BF16)
        gbc = sbt("gbc", [128, D])
        bbc = sbt("bbc", [128, D])
        ccbc = sbt("ccbc", [128, D])
        csbc = sbt("csbc", [128, D])
        lamc = sbt("lamc", [128, NDT])
        omlc = sbt("omlc", [128, NDT])
        omlbb = sbt("omlbb", [128, NDT])
        bbcol = sbt("bbcol", [128, NDT])
        idf = sbt("idf", [128, 128])
        idb = sbt("idb", [128, 128], BF16)
        wNc = sbt("wNc", [128, 1])
        wSc = sbt("wSc", [128, 1])
        cxc = sbt("cxc", [128, 1])
        i127 = sbt("i127", [128, 1])
        zcol = sbt("zcol", [128, 1])
        epscol = sbt("epscol", [128, 1])
        csc = sbt("csc", [128, NDT])
        icsc = sbt("icsc", [128, NDT])
        iotaA = sbt("iotaA", [128, 128])
        iotaB = sbt("iotaB", [128, 128])
        junk = sbt("junk", [128, D])
        stats = [sbt(f"stats{p}", [128, 64]) for p in range(2)]

        SC = {}
        ofs = 0
        for nm, w in [("szn", 2 * NTT), ("sz", NTT), ("szq", NTT), ("va1", NTT),
                      ("rs1", NTT), ("nm1", NTT),
                      ("sr1", NTT), ("srq", NTT), ("va2", NTT), ("rs2", NTT),
                      ("nm2", NTT),
                      ("am", NTT), ("sro", NTT), ("ivs", NTT),
                      ("sr2", NTT), ("sr2q", NTT), ("va3", NTT), ("rs3", NTT),
                      ("nm3", NTT)]:
            SC[nm] = (ofs, w)
            ofs += w
        assert ofs <= 64

        big1 = sbt("big1", [128, 8192])
        big2 = sbt("big2", [128, 8192])
        big3 = sbt("big3", [128, 8192], BF16)
        big4 = sbt("big4", [128, 8192], BF16)
        Ab = sbt("Ab", [128, 2048])

        wst = big1
        x_ = [big1[:, 0:2048], big1[:, 2048:4096]]
        vh = [big1[:, 4096:6144], big1[:, 6144:8192]]
        rtst = big2
        A_ = [big2[:, 0:2048], big2[:, 2048:4096]]
        B_ = [big2[:, 4096:6144], big2[:, 6144:8192]]
        qrsT = big3
        q_ = [big3[:, 0:2048], big3[:, 2048:4096]]
        qT = [big3[:, 4096:6144], big3[:, 6144:8192]]
        xh = [big4[:, 0:2048], big4[:, 2048:4096]]
        xl = [big4[:, 4096:6144], big4[:, 6144:8192]]

        pstx = [E(nc.psum_tensor(f"pstx{i}", [128, 512], F32)) for i in range(2)]
        psm1 = [E(nc.psum_tensor(f"psm1{i}", [128, 512], F32)) for i in range(2)]
        psm2 = [E(nc.psum_tensor(f"psm2{i}", [128, 512], F32)) for i in range(2)]
        pstq = E(nc.psum_tensor("pstq", [128, 1024], BF16))
        psm3 = E(nc.psum_tensor("psm3", [128, 512], F32))

        sems = {}
        for en in Planner.ENGINES:
            sems[en] = E(nc.semaphore("s_" + en))
        for en in ("sync", "gpsimd"):
            for r in range(Planner.NDMA):
                sems[f"{en}_dma{r}"] = E(nc.semaphore(f"s_{en}_dma{r}"))

        P = Planner()

        P.step("gpsimd", lambda e: e.iota(iotaA[:], [[1, 128]], channel_multiplier=0,
                                          allow_small_or_imprecise_dtypes=True),
               writes=["iotaA"])
        P.step("gpsimd", lambda e: e.iota(iotaB[:], [[0, 128]], channel_multiplier=1,
                                          allow_small_or_imprecise_dtypes=True),
               writes=["iotaB"])
        P.step("vector", lambda e: e.tensor_tensor(idf[:], iotaA[:], iotaB[:], OP.is_equal),
               reads=["iotaA", "iotaB"], writes=["idf"])
        P.step("vector", lambda e: e.tensor_copy(idb[:], idf[:]), reads=["idf"], writes=["idb"])
        P.step("vector", lambda e: e.memset(wSc[:], 1.0), writes=["wSc"])
        P.step("vector", lambda e: e.memset(zcol[:], 0.0), writes=["zcol"])
        P.step("vector", lambda e: e.memset(epscol[:], LN_EPS), writes=["epscol"])
        P.step("vector", lambda e: e.memset(i127[:], 127.0), writes=["i127"])
        P.step("vector", lambda e: e.tensor_scalar(wSc[:], wSc[:], C52, C52, OP.add, OP.subtract),
               reads=["wSc"], writes=["wSc"])
        P.step("vector", lambda e: e.reciprocal(i127[:], i127[:]), reads=["i127"], writes=["i127"])
        P.step("vector", lambda e: e.tensor_scalar(wNc[:], wSc[:], -1.0, 1.0, OP.mult, OP.add),
               reads=["wSc"], writes=["wNc"])
        P.step("vector", lambda e: e.tensor_scalar(cxc[:], wSc[:], C52, C23, OP.mult, OP.bypass),
               reads=["wSc"], writes=["cxc"])
        P.step("vector", lambda e: e.scalar_tensor_tensor(cxc[:], wNc[:], C23, cxc[:], OP.mult, OP.add),
               reads=["wNc", "cxc"], writes=["cxc"])

        P.step("sync", lambda e: e.dma_start(
            lamc[:], lrd.ap().rearrange("(t p) -> p t", p=128),
            allow_slow_non_contiguous=True), writes=["lamc"], dma=True)
        P.step("sync", lambda e: e.dma_start(
            bbcol[:], bbd.ap().rearrange("(t p) -> p t", p=128),
            allow_slow_non_contiguous=True), writes=["bbcol"], dma=True)
        P.step("sync", lambda e: e.dma_start(
            gbc[:], bass.AP(gad, 0, [[0, 128], [1, D]])), writes=["gbc"], dma=True)
        P.step("sync", lambda e: e.dma_start(
            bbc[:], bass.AP(bed, 0, [[0, 128], [1, D]])), writes=["bbc"], dma=True)
        P.step("sync", lambda e: e.dma_start(
            ccbc[:], bass.AP(bcd, 0, [[0, 128], [1, D]])), writes=["ccbc"], dma=True)

        P.step("scalar", lambda e: e.activation(lamc[:], lamc[:], AF.Sigmoid),
               reads=["lamc"], writes=["lamc"])
        P.step("vector", lambda e: e.tensor_scalar(omlc[:], lamc[:], -1.0, -1.0,
                                                   OP.mult, OP.subtract),
               reads=["lamc"], writes=["omlc"])
        P.step("vector", lambda e: e.tensor_tensor(omlbb[:], omlc[:], bbcol[:], OP.mult),
               reads=["omlc", "bbcol"], writes=["omlbb"])

        def w_load(dram):
            P.step("sync", lambda e, d=dram: e.dma_start(
                wst[:], bass.AP(d, 0, [[D, 128], [128 * D, NDT], [1, D]])),
                writes=["wst"], dma=True)

        def w_split(hi, lo, hik, lok):
            P.step("scalar", lambda e, h=hi: e.copy(h[:], wst[:]),
                   reads=["wst"], writes=[hik])
            P.step("vector", lambda e, h=hi, l=lo: e.tensor_tensor(
                l[:], wst[:], h[:], OP.subtract),
                reads=["wst", hik], writes=[lok])

        w_load(wbTd)
        w_split(wbh, wbl, "wbh", "wbl")
        w_load(wcTd)
        w_split(wch, wcl, "wch", "wcl")
        w_load(rhsd)

        for jj in range(NDT):
            bk = jj % 2
            for kk in range(NDT):
                P.step("tensor", lambda e, j=jj, k=kk, b=bk: e.transpose(
                    pstx[b][:, (k % 4) * 128:(k % 4) * 128 + 128],
                    wst[:, k * D + j * 128: k * D + (j + 1) * 128], idf[:]),
                    reads=["wst", "idf"], writes=[f"pstx{bk}"])
                if kk % 4 == 3:
                    P.step("scalar", lambda e, j=jj, k=kk, b=bk: e.copy(
                        rtst[:, j * D + (k - 3) * 128: j * D + (k + 1) * 128],
                        pstx[b][:, 0:512]),
                        reads=[f"pstx{bk}"], writes=[f"rtst{jj}"])
        for jj in range(NDT):
            P.step("vector", lambda e, j=jj: e.tensor_reduce(
                csc[:, j:j + 1], rtst[:, j * D:(j + 1) * D], AX.X, OP.max,
                apply_absolute_value=True), reads=[f"rtst{jj}"], writes=["csc"])
        P.step("vector", lambda e: e.tensor_scalar(csc[:], csc[:], 1e-12, i127[:],
                                                   OP.max, OP.mult),
               reads=["csc", "i127"], writes=["csc"])
        P.step("vector", lambda e: e.tensor_copy(icsc[:], csc[:]), reads=["csc"], writes=["icsc"])
        P.step("vector", lambda e: e.reciprocal(icsc[:], icsc[:]), reads=["icsc"], writes=["icsc"])
        for jj in range(NDT):
            sl = slice(jj * D, (jj + 1) * D)
            P.step("scalar", lambda e, s=sl, j=jj: e.activation(
                rtst[:, s], rtst[:, s], AF.Identity, bias=zcol[:], scale=icsc[:, j:j + 1]),
                reads=[f"rtst{jj}", "icsc", "zcol"], writes=[f"rtst{jj}"])
            P.step("vector", lambda e, s=sl: e.tensor_scalar(
                rtst[:, s], rtst[:, s], cxc[:], cxc[:], OP.add, OP.subtract),
                reads=[f"rtst{jj}", "cxc"], writes=[f"rtst{jj}"])
            P.step("vector", lambda e, s=sl: e.tensor_scalar(
                qrsT[:, s], rtst[:, s], 127.0, -127.0, OP.min, OP.max),
                reads=[f"rtst{jj}"], writes=["qrsT"])
        for jj in range(NDT):
            for kk in range(NDT):
                P.step("tensor", lambda e, j=jj, k=kk: e.transpose(
                    pstq[:, k * 128:(k + 1) * 128],
                    qrsT[:, k * D + j * 128: k * D + (j + 1) * 128], idb[:]),
                    reads=["qrsT", "idb"], writes=["pstq"])
            P.step("scalar", lambda e, j=jj: e.copy(qrs[:, j * D:(j + 1) * D], pstq[:]),
                   reads=["pstq"], writes=["qrs"])
        P.step("gpsimd", lambda e: e.dma_start(
            bass.AP(csrow_d, 0, [[1, 128], [128, NDT]]), csc[:],
            allow_slow_non_contiguous=True), reads=["csc"], writes=["csrowd"], dma=True)
        P.step("gpsimd", lambda e: e.dma_start(
            csbc[:], bass.AP(csrow_d, 0, [[0, 128], [1, D]])),
            reads=["csrowd"], writes=["csbc"], dma=True)

        P.alias(["x0t0", "x0t1", "x1t0", "x1t1", "vh0", "vh1"], ["wst"])
        P.alias(["A0", "A1", "B0", "B1"], [f"rtst{j}" for j in range(NDT)])
        P.alias(["q0", "q1", "qT0", "qT1"], ["qrsT"])

        def sc_of(p):
            stt = stats[p]
            def sc(nm, i0=0, w=None):
                o, wd = SC[nm]
                w = wd if w is None else w
                return stt[:, o + i0: o + i0 + w]
            return sc

        def x_load(c, i=None):
            p = c & 1
            tiles = range(NTT) if i is None else [i]
            for ii_ in tiles:
                P.step("sync", lambda e, t=c * CT + ii_ * 128, pp=p, jj=ii_:
                       e.dma_start(
                    x_[pp][:, jj * D:(jj + 1) * D],
                    bass.AP(xd, t * D, [[D, 128], [1, D]])),
                    writes=[f"x{p}t{ii_}"], dma=True)

        def tx_pair(c, g):
            p = c & 1
            bk = g % 2
            xk0, xk1, xhk, xlk = f"x{p}t0", f"x{p}t1", f"xh{p}", f"xl{p}"
            items = []
            for o in range(4):
                jj, ii = 2 * g + o // 2, o % 2
                items.append((lambda e, j=jj, i=ii, oo=o, b=bk, pp=p: e.transpose(
                    pstx[b][:, oo * 128:(oo + 1) * 128],
                    x_[pp][:, i * D + j * 128: i * D + (j + 1) * 128], idf[:]),
                    [xk0 if ii == 0 else xk1, "idf"], [f"pstx{bk}"]))
            P.step_multi("tensor", items)
            P.step("scalar", lambda e, gg=g, b=bk, pp=p: e.copy(
                xh[pp][:, 2 * gg * CT: 2 * (gg + 1) * CT], pstx[b][:, 0:512]),
                reads=[f"pstx{bk}"], writes=[xhk])
            P.step("vector", lambda e, gg=g, b=bk, pp=p: e.tensor_tensor(
                xl[pp][:, 2 * gg * CT: 2 * (gg + 1) * CT], pstx[b][:, 0:512],
                xh[pp][:, 2 * gg * CT: 2 * (gg + 1) * CT], OP.subtract),
                reads=[f"pstx{bk}", xhk], writes=[xlk])

        def tq_pair(c, g):
            p = c & 1
            qk, qTk = f"q{p}", f"qT{p}"
            items = []
            for o in range(4):
                jj, ii = 2 * g + o // 2, o % 2
                items.append((lambda e, j=jj, i=ii, oo=o, pp=p: e.transpose(
                    pstq[:, oo * 128:(oo + 1) * 128],
                    q_[pp][:, i * D + j * 128: i * D + (j + 1) * 128], idb[:]),
                    [qk, "idb"], ["pstq"]))
            P.step_multi("tensor", items)
            P.step("scalar", lambda e, gg=g, pp=p: e.copy(
                qT[pp][:, 2 * gg * CT: 2 * (gg + 1) * CT], pstq[:, 0:512]),
                reads=["pstq"], writes=[qTk])

        def mm2_group(c, i, n):
            p = c & 1
            bk = (i * 2 + n) % 2
            xhk, xlk, Ak, Bk = f"xh{p}", f"xl{p}", f"A{p}", f"B{p}"
            stk = f"stats{p}"
            sc = sc_of(p)
            combos = [(xh, wch, xhk, "wch"), (xh, wcl, xhk, "wcl"),
                      (xl, wch, xlk, "wch")]
            nmm = 3 * NDT
            items = []
            idx = 0
            for (ht, wt, hk_, wk_) in combos:
                for k in range(NDT):
                    items.append((lambda e, ii=i, nn=n, kk=k, b=bk, hh=ht,
                                  ww=wt, pp=p, st=(idx == 0), sp=(idx == nmm - 1):
                                  e.matmul(
                        psm2[b][:, 0:512],
                        hh[pp][:, kk * CT + ii * 128: kk * CT + (ii + 1) * 128],
                        ww[:, kk * D + nn * 512: kk * D + (nn + 1) * 512],
                        start=st, stop=sp),
                        [hk_, wk_], [f"psm2{bk}"]))
                    idx += 1
            P.step_multi("tensor", items)

        def mm2_evac(c, i, n):
            p = c & 1
            bk = (i * 2 + n) % 2
            Ak, Bk = f"A{p}", f"B{p}"
            stk = f"stats{p}"
            sc = sc_of(p)
            P.step("vector", lambda e, ii=i, nn=n, b=bk, pp=p,
                   acc=sc("szn", i * 2 + n, 1): e.scalar_tensor_tensor(
                B_[pp][:, ii * D + nn * 512: ii * D + (nn + 1) * 512],
                psm2[b][:, 0:512], 1.0,
                A_[pp][:, ii * D + nn * 512: ii * D + (nn + 1) * 512],
                OP.mult, OP.add, accum_out=acc),
                reads=[f"psm2{bk}", Ak], writes=[Bk, stk])

        def mid_a(c):
            p = c & 1
            for i in range(NTT):
                P.step("gpsimd", lambda e, ii=i, pp=p: e.tensor_tensor(
                    A_[pp][:, ii * D:(ii + 1) * D], x_[pp][:, ii * D:(ii + 1) * D],
                    ccbc[:], OP.add),
                    reads=[f"x{p}t{i}", "ccbc"], writes=[f"A{p}"])
            for i in range(NTT):
                P.step("gpsimd", lambda e, ii=i, pp=p: e.tensor_tensor(
                    x_[pp][:, ii * D:(ii + 1) * D], x_[pp][:, ii * D:(ii + 1) * D],
                    bbc[:], OP.add),
                    reads=[f"x{p}t{i}", "bbc"], writes=[f"x{p}t{i}"])

        def mm3_group(c, i, n):
            p = c & 1
            qTk = f"qT{p}"
            P.step_multi("tensor", [
                (lambda e, ii=i, nn=n, kk=k, pp=p,
                  st=(k == 0), sp=(k == NDT - 1): e.matmul(
                    psm3[:, 0:512],
                    qT[pp][:, kk * CT + ii * 128: kk * CT + (ii + 1) * 128],
                    qrs[:, kk * D + nn * 512: kk * D + (nn + 1) * 512],
                    start=st, stop=sp),
                 [qTk, "qrs"], ["psm3"])
                for k in range(NDT)])
            sc = sc_of(p)
            P.step("scalar", lambda e, ii=i, nn=n, s=sc("sro", i, 1):
                   e.activation(
                Ab[:, ii * D + nn * 512: ii * D + (nn + 1) * 512],
                psm3[:, 0:512], AF.Identity, bias=zcol[:], scale=s),
                reads=["psm3", f"stats{p}", "zcol"], writes=["Ab"])

        def back_csr(c, i):
            p = c & 1
            sc = sc_of(p)
            P.step("gpsimd", lambda e, ii=i: e.tensor_tensor(
                Ab[:, ii * D:(ii + 1) * D], Ab[:, ii * D:(ii + 1) * D],
                csbc[:], OP.mult), reads=["Ab", "csbc"], writes=["Ab"])
            P.step("vector", lambda e, ii=i, pp=p, acc=sc("sr2", i, 1):
                   e.scalar_tensor_tensor(
                Ab[:, ii * D:(ii + 1) * D], Ab[:, ii * D:(ii + 1) * D], 1.0,
                B_[pp][:, ii * D:(ii + 1) * D], OP.mult, OP.add, accum_out=acc),
                reads=["Ab", f"B{p}"], writes=["Ab", f"stats{p}"])

        def ln_stats_steps(p, sumk, sqk, vak, rsk, nmk, i=None):
            stk = f"stats{p}"
            sc = sc_of(p)
            if i is None:
                i0, w = 0, None
            else:
                i0, w = i, 1
            P.step("vector", lambda e, o=sc(vak, i0, w), a=sc(sumk, i0, w): e.tensor_tensor(
                o, a, a, OP.mult), reads=[stk], writes=[stk])
            P.step("vector", lambda e, o=sc(vak, i0, w), b=sc(sqk, i0, w): e.scalar_tensor_tensor(
                o, o, -1.0 / D, b, OP.mult, OP.add), reads=[stk], writes=[stk])
            P.step("scalar", lambda e, o=sc(vak, i0, w): e.activation(
                o, o, AF.Ln, bias=epscol[:], scale=1.0 / D),
                reads=[stk, "epscol"], writes=[stk])
            P.step("scalar", lambda e, o=sc(rsk, i0, w), a=sc(vak, i0, w): e.activation(
                o, a, AF.Exp, bias=0.0, scale=-0.5), reads=[stk], writes=[stk])
            P.step("vector", lambda e, o=sc(nmk, i0, w), a=sc(sumk, i0, w), b=sc(rsk, i0, w):
                   e.scalar_tensor_tensor(o, a, -1.0 / D, b, OP.mult, OP.mult),
                   reads=[stk], writes=[stk])

        def ln3a_i(c, i):
            p = c & 1
            stk = f"stats{p}"
            sc = sc_of(p)
            P.step("scalar", lambda e, ii=i, pp=p, acc=sc("sr2q", i, 1): e.activation(
                B_[pp][:, ii * D:(ii + 1) * D], Ab[:, ii * D:(ii + 1) * D],
                AF.Square, accum_out=acc),
                reads=["Ab"], writes=[f"B{p}", stk])
            ln_stats_steps(p, "sr2", "sr2q", "va3", "rs3", "nm3", i)

        def ln3b_i(c, i):
            p = c & 1
            Bk = f"B{p}"
            stk = f"stats{p}"
            sc = sc_of(p)
            P.step("scalar", lambda e, ii=i, pp=p, b=sc("nm3", i, 1),
                   s=sc("rs3", i, 1): e.activation(
                B_[pp][:, ii * D:(ii + 1) * D], Ab[:, ii * D:(ii + 1) * D],
                AF.Identity, bias=b, scale=s),
                reads=["Ab", stk], writes=[Bk])

        def y_store(c):
            p = c & 1
            t0 = c * CT
            P.step("gpsimd", lambda e, t=t0, pp=p: e.dma_start(
                bass.AP(yd, t * D, [[D, 128], [128 * D, NTT], [1, D]]), B_[pp][:]),
                reads=[f"B{p}"], dma=True)

        def mm1_group(c, g):
            p = c & 1
            bk = g % 2
            xhk, xlk = f"xh{p}", f"xl{p}"
            combos = [(wbh, xh, "wbh", xhk), (wbl, xh, "wbl", xhk),
                      (wbh, xl, "wbh", xlk)]
            nmm = 3 * NDT * 2
            items = []
            idx = 0
            for (wt, xt, wk_, xk_) in combos:
                for k in range(NDT):
                    for jo in range(2):
                        jj_ = 2 * g + jo
                        items.append((lambda e, jj=jj_, kk=k, b=bk, w=wt, xx=xt,
                                      pp=p, oo=jo,
                                      st=(idx == 0), sp=(idx == nmm - 1): e.matmul(
                            psm1[b][:, oo * CT:(oo + 1) * CT],
                            w[:, kk * D + jj * 128: kk * D + (jj + 1) * 128],
                            xx[pp][:, kk * CT:(kk + 1) * CT],
                            start=st, stop=sp),
                            [wk_, xk_], [f"psm1{bk}"]))
                        idx += 1
            P.step_multi("tensor", items)

        def v_evac(c, g):
            p = c & 1
            bk = g % 2
            for jo in range(2):
                jj_ = 2 * g + jo
                P.step("scalar", lambda e, jj=jj_, b=bk, pp=p, oo=jo: e.activation(
                    vh[pp][:, jj * CT:(jj + 1) * CT],
                    psm1[b][:, oo * CT:(oo + 1) * CT], AF.Identity,
                    bias=omlbb[:, jj:jj + 1], scale=omlc[:, jj:jj + 1]),
                    reads=[f"psm1{bk}", "omlbb", "omlc"], writes=[f"vh{p}"])

        def scan_j(c, j):
            p = c & 1
            vhk = f"vh{p}"
            if c == 0:
                init = 0.0
                rd = [vhk, "lamc"]
            else:
                init = vh[1 - p][:, j * CT + CT - 1: j * CT + CT]
                rd = [vhk, "lamc", f"vh{1 - p}"]
            P.step("vector", lambda e, jj=j, pp=p, ini=init: e.tensor_tensor_scan(
                vh[pp][:, jj * CT:(jj + 1) * CT],
                lamc[:, jj:jj + 1].broadcast_to([128, CT]),
                vh[pp][:, jj * CT:(jj + 1) * CT],
                ini, OP.mult, OP.add),
                reads=rd, writes=[vhk])

        def h_split(c):
            p = c & 1
            vhk = f"vh{p}"
            xhk, xlk = f"xh{p}", f"xl{p}"
            P.step("scalar", lambda e, pp=p: e.copy(xh[pp][:], vh[pp][:]),
                   reads=[vhk], writes=[xhk])
            P.step("gpsimd", lambda e, pp=p: e.tensor_tensor(
                xl[pp][:], vh[pp][:], xh[pp][:], OP.subtract),
                reads=[vhk, xhk], writes=[xlk])

        def p1a_i(c, i):
            p = c & 1
            Ak, Bk = f"A{p}", f"B{p}"
            stk = f"stats{p}"
            stt = stats[p]
            sc = sc_of(p)
            P.step("vector", lambda e, o=sc("sz", i, 1),
                   a=sc("szn", 2 * i, 1), b=sc("szn", 2 * i + 1, 1):
                   e.tensor_tensor(o, a, b, OP.add),
                   reads=[stk], writes=[stk])
            P.step("scalar", lambda e, ii=i, pp=p, acc=sc("szq", i, 1): e.activation(
                A_[pp][:, ii * D:(ii + 1) * D], B_[pp][:, ii * D:(ii + 1) * D],
                AF.Square, accum_out=acc),
                reads=[Bk], writes=[Ak, stk])
            ln_stats_steps(p, "sz", "szq", "va1", "rs1", "nm1", i)

        def p1b_i(c, i):
            p = c & 1
            xk, Ak, Bk = f"x{p}t{i}", f"A{p}", f"B{p}"
            stk = f"stats{p}"
            sc = sc_of(p)
            P.step("scalar", lambda e, ii=i, pp=p, b=sc("nm1", i, 1),
                   s=sc("rs1", i, 1): e.activation(
                A_[pp][:, ii * D:(ii + 1) * D], B_[pp][:, ii * D:(ii + 1) * D],
                AF.Identity, bias=b, scale=s),
                reads=[Bk, stk], writes=[Ak])
            P.step("vector", lambda e, ii=i, pp=p: e.tensor_tensor(
                A_[pp][:, ii * D:(ii + 1) * D], A_[pp][:, ii * D:(ii + 1) * D],
                gbc[:], OP.mult), reads=[Ak, "gbc"], writes=[Ak])
            P.step("vector", lambda e, ii=i, pp=p, acc=sc("sr1", i, 1):
                   e.scalar_tensor_tensor(
                A_[pp][:, ii * D:(ii + 1) * D], A_[pp][:, ii * D:(ii + 1) * D], 1.0,
                x_[pp][:, ii * D:(ii + 1) * D], OP.mult, OP.add, accum_out=acc),
                reads=[Ak, xk], writes=[Ak, stk])
        def p2_i(c, i):
            p = c & 1
            Ak, Bk, qk = f"A{p}", f"B{p}", f"q{p}"
            stk = f"stats{p}"
            sc = sc_of(p)
            P.step("scalar", lambda e, ii=i, pp=p, acc=sc("srq", i, 1): e.activation(
                B_[pp][:, ii * D:(ii + 1) * D], A_[pp][:, ii * D:(ii + 1) * D],
                AF.Square, accum_out=acc),
                reads=[Ak], writes=[Bk, stk])
            ln_stats_steps(p, "sr1", "srq", "va2", "rs2", "nm2", i)
            P.step("scalar", lambda e, ii=i, pp=p, b=sc("nm2", i, 1),
                   s=sc("rs2", i, 1): e.activation(
                B_[pp][:, ii * D:(ii + 1) * D], A_[pp][:, ii * D:(ii + 1) * D],
                AF.Identity, bias=b, scale=s),
                reads=[Ak, stk], writes=[Bk])
            P.step("vector", lambda e, ii=i, pp=p, o=sc("am", i, 1): e.tensor_reduce(
                o, B_[pp][:, ii * D:(ii + 1) * D], AX.X, OP.max,
                apply_absolute_value=True), reads=[Bk], writes=[stk])
            P.step("vector", lambda e, o=sc("sro", i, 1), a=sc("am", i, 1): e.tensor_scalar(
                o, a, 1e-12, i127[:], OP.max, OP.mult),
                reads=[stk, "i127"], writes=[stk])
            P.step("vector", lambda e, o=sc("ivs", i, 1), a=sc("sro", i, 1): e.reciprocal(o, a),
                   reads=[stk], writes=[stk])
            sl = slice(i * D, (i + 1) * D)
            P.step("scalar", lambda e, s=sl, pp=p, sv=sc("ivs", i, 1): e.activation(
                A_[pp][:, s], B_[pp][:, s], AF.Identity, bias=zcol[:], scale=sv),
                reads=[Bk, stk, "zcol"], writes=[Ak])
            P.step("vector", lambda e, s=sl, pp=p: e.tensor_scalar(
                q_[pp][:, s], A_[pp][:, s], cxc[:], cxc[:], OP.add, OP.subtract),
                reads=[Ak, "cxc"], writes=[qk])

        for k in range(NCH + 2):
            if k == 0:
                x_load(0)
                x_load(1)
            if 1 <= k <= NCH:
                mid_a(k - 1)
            if k < NCH:
                for g in range(4):
                    tx_pair(k, g)
                    if 2 <= k:
                        tq_pair(k - 2, g)
            elif 2 <= k:
                for g in range(4):
                    tq_pair(k - 2, g)
            m2 = [(i, n) for i in range(NTT) for n in range(2)] if 1 <= k <= NCH else []
            m3 = [(i, n) for i in range(NTT) for n in range(2)] if 2 <= k else []
            for gi in range(max(len(m2), len(m3))):
                if gi < len(m2):
                    mm2_group(k - 1, *m2[gi])
                    mm2_evac(k - 1, *m2[gi])
                if gi < len(m3):
                    mm3_group(k - 2, *m3[gi])
                    if m3[gi][1] == 1:
                        back_csr(k - 2, m3[gi][0])
                if gi == 1 and 1 <= k <= NCH:
                    p1a_i(k - 1, 0)
                if gi == 3 and 1 <= k <= NCH:
                    p1a_i(k - 1, 1)
            def mm1_pair(g):
                if k < NCH:
                    mm1_group(k, g)
                    v_evac(k, g)
                    if g >= 1:
                        scan_j(k, 2 * g - 2)
                        scan_j(k, 2 * g - 1)
            mm1_pair(0)
            if 1 <= k <= NCH:
                p1b_i(k - 1, 0)
            if 1 <= k and 2 <= k + 1 < NCH:
                x_load(k + 1, 0)
            if 2 <= k:
                ln3a_i(k - 2, 0)
            mm1_pair(1)
            if 1 <= k <= NCH:
                p2_i(k - 1, 0)
                p1b_i(k - 1, 1)
            if 1 <= k and 2 <= k + 1 < NCH:
                x_load(k + 1, 1)
            if 2 <= k:
                ln3a_i(k - 2, 1)
            mm1_pair(2)
            if 2 <= k:
                ln3b_i(k - 2, 0)
                ln3b_i(k - 2, 1)
            mm1_pair(3)
            if 1 <= k <= NCH:
                p2_i(k - 1, 1)
            if 2 <= k:
                y_store(k - 2)
            if k < NCH:
                scan_j(k, 6)
                scan_j(k, 7)
                h_split(k)

        n_out_dma = P.dma_idx["gpsimd"]
        with nc.Block() as blk:
            @blk.sync
            def _(e):
                P.replay("sync", e, sems)

            @blk.gpsimd
            def _(e):
                P.replay("gpsimd", e, sems)
                for r in range(Planner.NDMA):
                    uses = (n_out_dma - 1 - r) // Planner.NDMA + 1 if r < n_out_dma else 0
                    if uses > 0:
                        e.wait_ge(sems[f"gpsimd_dma{r}"], 16 * uses)

            @blk.tensor
            def _(e):
                P.replay("tensor", e, sems)

            @blk.vector
            def _(e):
                P.replay("vector", e, sems)

            @blk.scalar
            def _(e):
                P.replay("scalar", e, sems)

    return nc


def kernel(L0_attn_x, L0_attn_log_lambda_raw, L0_attn_Wb, L0_attn_bb,
           L0_attn_Wc, L0_attn_bc, L0_attn_gamma, L0_attn_beta, L0_ffn_rhs):
    B, T, _ = L0_attn_x.shape
    nc = build_nc(T=T)
    nc.finalize()
    base = {
        "wbT": np.ascontiguousarray(np.asarray(L0_attn_Wb).T),
        "wcT": np.ascontiguousarray(np.asarray(L0_attn_Wc).T),
        "rhs": np.ascontiguousarray(np.asarray(L0_ffn_rhs)),
        "lraw": np.ascontiguousarray(np.asarray(L0_attn_log_lambda_raw)),
        "bb": np.ascontiguousarray(np.asarray(L0_attn_bb)),
        "bc": np.ascontiguousarray(np.asarray(L0_attn_bc)),
        "gamma": np.ascontiguousarray(np.asarray(L0_attn_gamma)),
        "beta": np.ascontiguousarray(np.asarray(L0_attn_beta)),
    }
    in_maps = [dict(base, x=np.ascontiguousarray(np.asarray(L0_attn_x)[b]))
               for b in range(B)]
    res = run_bass_kernel_spmd(nc, in_maps, core_ids=list(range(B)))
    return np.stack([np.asarray(res.results[b]["y"]) for b in range(B)], axis=0)


# version 41
# speedup vs baseline: 1.0229x; 1.0229x over previous
import numpy as np
import concourse.bass as bass
import concourse.mybir as mybir
from concourse.bass_utils import run_bass_kernel_spmd

F32 = mybir.dt.float32
BF16 = mybir.dt.bfloat16
AF = mybir.ActivationFunctionType
OP = mybir.AluOpType
AX = mybir.AxisListType

C23 = 12582912.0
C52 = 6755399441055744.0
LN_EPS = 1e-5
D = 1024
NDT = D // 128


class Planner:

    ENGINES = ("sync", "gpsimd", "tensor", "vector", "scalar")
    NDMA = 8

    def __init__(self):
        self.steps = {e: [] for e in self.ENGINES}
        self.count = {e: 0 for e in self.ENGINES}
        self.dma_idx = {e: 0 for e in self.ENGINES}
        self.water = {e: {} for e in self.ENGINES}
        self.writer = {}
        self.readers = {}

    def _need(self, eng, tag, needs):
        if tag is None:
            return
        sem, val = tag
        if self.water[eng].get(sem, 0) >= val:
            return
        if needs.get(sem, 0) < val:
            needs[sem] = val

    def step(self, eng, fn, reads=(), writes=(), dma=False):
        needs = {}
        for k in reads:
            self._need(eng, self.writer.get(k), needs)
        for k in writes:
            self._need(eng, self.writer.get(k), needs)
            for t in self.readers.get(k, ()):
                self._need(eng, t, needs)
        if dma:
            i = self.dma_idx[eng]
            r, uses = i % self.NDMA, i // self.NDMA
            dsem = f"{eng}_dma{r}"
            if uses > 0:
                self._need(eng, (dsem, 16 * uses), needs)
            self.dma_idx[eng] = i + 1
            tag = (dsem, 16 * (uses + 1))
        else:
            self.count[eng] += 1
            tag = (eng, self.count[eng])
        waits = sorted(needs.items())
        for sem, val in waits:
            self.water[eng][sem] = val
        for k in reads:
            self.readers.setdefault(k, []).append(tag)
        for k in writes:
            self.writer[k] = tag
            self.readers[k] = []
        self.steps[eng].append((waits, fn, dma, tag))
        return tag

    def step_multi(self, eng, items, dma=False):
        assert not dma
        needs = {}
        for fn, reads, writes in items:
            for k in reads:
                self._need(eng, self.writer.get(k), needs)
            for k in writes:
                self._need(eng, self.writer.get(k), needs)
                for t in self.readers.get(k, ()):
                    self._need(eng, t, needs)
        self.count[eng] += 1
        tag = (eng, self.count[eng])
        waits = sorted(needs.items())
        for sem, val in waits:
            self.water[eng][sem] = val
        for fn, reads, writes in items:
            for k in reads:
                self.readers.setdefault(k, []).append(tag)
            for k in writes:
                self.writer[k] = tag
                self.readers[k] = []
        self.steps[eng].append((waits, [f for f, _, _ in items], False, tag))
        return tag

    def alias(self, dst_keys, src_keys):
        tags = []
        for k in src_keys:
            if k in self.writer and self.writer[k] is not None:
                tags.append(self.writer[k])
            tags.extend(self.readers.get(k, ()))
        for k in dst_keys:
            self.writer[k] = None
            self.readers[k] = list(tags)

    def replay(self, eng_name, e, sems):
        for waits, fn, dma, tag in self.steps[eng_name]:
            for sem_name, val in waits:
                e.wait_ge(sems[sem_name], val)
            if isinstance(fn, list):
                for f in fn[:-1]:
                    f(e)
                ins = fn[-1](e)
            else:
                ins = fn(e)
            ins.then_inc(sems[tag[0]], 16 if dma else 1)


def build_nc(T=8192, CT=256):
    NCH = T // CT
    NTT = CT // 128
    nc = bass.Bass("TRN2", target_bir_lowering=False)

    xd = nc.dram_tensor("x", [T, D], F32, kind="ExternalInput")
    wbTd = nc.dram_tensor("wbT", [D, D], F32, kind="ExternalInput")
    wcTd = nc.dram_tensor("wcT", [D, D], F32, kind="ExternalInput")
    rhsd = nc.dram_tensor("rhs", [D, D], F32, kind="ExternalInput")
    lrd = nc.dram_tensor("lraw", [D], F32, kind="ExternalInput")
    bbd = nc.dram_tensor("bb", [D], F32, kind="ExternalInput")
    bcd = nc.dram_tensor("bc", [D], F32, kind="ExternalInput")
    gad = nc.dram_tensor("gamma", [D], F32, kind="ExternalInput")
    bed = nc.dram_tensor("beta", [D], F32, kind="ExternalInput")
    yd = nc.dram_tensor("y", [T, D], F32, kind="ExternalOutput")
    csrow_d = nc.dram_tensor("csrow", [D], F32)

    import contextlib
    with contextlib.ExitStack() as stk:
        E = stk.enter_context

        def sbt(name, shape, dt=F32):
            return E(nc.sbuf_tensor(name, shape, dt))

        wbh = sbt("wbh", [128, NDT * D], BF16)
        wbl = sbt("wbl", [128, NDT * D], BF16)
        wch = sbt("wch", [128, NDT * D], BF16)
        wcl = sbt("wcl", [128, NDT * D], BF16)
        qrs = sbt("qrs", [128, NDT * D], BF16)
        gbc = sbt("gbc", [128, D])
        bbc = sbt("bbc", [128, D])
        ccbc = sbt("ccbc", [128, D])
        csbc = sbt("csbc", [128, D])
        lamc = sbt("lamc", [128, NDT])
        omlc = sbt("omlc", [128, NDT])
        omlbb = sbt("omlbb", [128, NDT])
        bbcol = sbt("bbcol", [128, NDT])
        idf = sbt("idf", [128, 128])
        idb = sbt("idb", [128, 128], BF16)
        wNc = sbt("wNc", [128, 1])
        wSc = sbt("wSc", [128, 1])
        cxc = sbt("cxc", [128, 1])
        i127 = sbt("i127", [128, 1])
        zcol = sbt("zcol", [128, 1])
        epscol = sbt("epscol", [128, 1])
        csc = sbt("csc", [128, NDT])
        icsc = sbt("icsc", [128, NDT])
        iotaA = sbt("iotaA", [128, 128])
        iotaB = sbt("iotaB", [128, 128])
        junk = sbt("junk", [128, D])
        stats = [sbt(f"stats{p}", [128, 64]) for p in range(2)]

        SC = {}
        ofs = 0
        for nm, w in [("szn", 2 * NTT), ("sz", NTT), ("szq", NTT), ("va1", NTT),
                      ("rs1", NTT), ("nm1", NTT),
                      ("sr1", NTT), ("srq", NTT), ("va2", NTT), ("rs2", NTT),
                      ("nm2", NTT),
                      ("am", NTT), ("sro", NTT), ("ivs", NTT),
                      ("sr2", NTT), ("sr2q", NTT), ("va3", NTT), ("rs3", NTT),
                      ("nm3", NTT)]:
            SC[nm] = (ofs, w)
            ofs += w
        assert ofs <= 64

        big1 = sbt("big1", [128, 8192])
        big2 = sbt("big2", [128, 8192])
        big3 = sbt("big3", [128, 8192], BF16)
        big4 = sbt("big4", [128, 8192], BF16)
        Ab = sbt("Ab", [128, 2048])

        wst = big1
        x_ = [big1[:, 0:2048], big1[:, 2048:4096]]
        vh = [big1[:, 4096:6144], big1[:, 6144:8192]]
        rtst = big2
        A_ = [big2[:, 0:2048], big2[:, 2048:4096]]
        B_ = [big2[:, 4096:6144], big2[:, 6144:8192]]
        qrsT = big3
        q_ = [big3[:, 0:2048], big3[:, 2048:4096]]
        qT = [big3[:, 4096:6144], big3[:, 6144:8192]]
        xh = [big4[:, 0:2048], big4[:, 2048:4096]]
        xl = [big4[:, 4096:6144], big4[:, 6144:8192]]

        pstx = [E(nc.psum_tensor(f"pstx{i}", [128, 512], F32)) for i in range(2)]
        psm1 = [E(nc.psum_tensor(f"psm1{i}", [128, 512], F32)) for i in range(2)]
        psm2 = [E(nc.psum_tensor(f"psm2{i}", [128, 512], F32)) for i in range(2)]
        pstq = E(nc.psum_tensor("pstq", [128, 1024], BF16))
        psm3 = E(nc.psum_tensor("psm3", [128, 512], F32))

        sems = {}
        for en in Planner.ENGINES:
            sems[en] = E(nc.semaphore("s_" + en))
        for en in ("sync", "gpsimd"):
            for r in range(Planner.NDMA):
                sems[f"{en}_dma{r}"] = E(nc.semaphore(f"s_{en}_dma{r}"))

        P = Planner()

        P.step("gpsimd", lambda e: e.iota(iotaA[:], [[1, 128]], channel_multiplier=0,
                                          allow_small_or_imprecise_dtypes=True),
               writes=["iotaA"])
        P.step("gpsimd", lambda e: e.iota(iotaB[:], [[0, 128]], channel_multiplier=1,
                                          allow_small_or_imprecise_dtypes=True),
               writes=["iotaB"])
        P.step("vector", lambda e: e.tensor_tensor(idf[:], iotaA[:], iotaB[:], OP.is_equal),
               reads=["iotaA", "iotaB"], writes=["idf"])
        P.step("vector", lambda e: e.tensor_copy(idb[:], idf[:]), reads=["idf"], writes=["idb"])
        P.step("vector", lambda e: e.memset(wSc[:], 1.0), writes=["wSc"])
        P.step("vector", lambda e: e.memset(zcol[:], 0.0), writes=["zcol"])
        P.step("vector", lambda e: e.memset(epscol[:], LN_EPS), writes=["epscol"])
        P.step("vector", lambda e: e.memset(i127[:], 127.0), writes=["i127"])
        P.step("vector", lambda e: e.tensor_scalar(wSc[:], wSc[:], C52, C52, OP.add, OP.subtract),
               reads=["wSc"], writes=["wSc"])
        P.step("vector", lambda e: e.reciprocal(i127[:], i127[:]), reads=["i127"], writes=["i127"])
        P.step("vector", lambda e: e.tensor_scalar(wNc[:], wSc[:], -1.0, 1.0, OP.mult, OP.add),
               reads=["wSc"], writes=["wNc"])
        P.step("vector", lambda e: e.tensor_scalar(cxc[:], wSc[:], C52, C23, OP.mult, OP.bypass),
               reads=["wSc"], writes=["cxc"])
        P.step("vector", lambda e: e.scalar_tensor_tensor(cxc[:], wNc[:], C23, cxc[:], OP.mult, OP.add),
               reads=["wNc", "cxc"], writes=["cxc"])

        P.step("sync", lambda e: e.dma_start(
            lamc[:], lrd.ap().rearrange("(t p) -> p t", p=128),
            allow_slow_non_contiguous=True), writes=["lamc"], dma=True)
        P.step("sync", lambda e: e.dma_start(
            bbcol[:], bbd.ap().rearrange("(t p) -> p t", p=128),
            allow_slow_non_contiguous=True), writes=["bbcol"], dma=True)
        P.step("sync", lambda e: e.dma_start(
            gbc[:], bass.AP(gad, 0, [[0, 128], [1, D]])), writes=["gbc"], dma=True)
        P.step("sync", lambda e: e.dma_start(
            bbc[:], bass.AP(bed, 0, [[0, 128], [1, D]])), writes=["bbc"], dma=True)
        P.step("sync", lambda e: e.dma_start(
            ccbc[:], bass.AP(bcd, 0, [[0, 128], [1, D]])), writes=["ccbc"], dma=True)

        P.step("scalar", lambda e: e.activation(lamc[:], lamc[:], AF.Sigmoid),
               reads=["lamc"], writes=["lamc"])
        P.step("vector", lambda e: e.tensor_scalar(omlc[:], lamc[:], -1.0, -1.0,
                                                   OP.mult, OP.subtract),
               reads=["lamc"], writes=["omlc"])
        P.step("vector", lambda e: e.tensor_tensor(omlbb[:], omlc[:], bbcol[:], OP.mult),
               reads=["omlc", "bbcol"], writes=["omlbb"])

        def w_load(dram):
            P.step("sync", lambda e, d=dram: e.dma_start(
                wst[:], bass.AP(d, 0, [[D, 128], [128 * D, NDT], [1, D]])),
                writes=["wst"], dma=True)

        def w_split(hi, lo, hik, lok):
            P.step("scalar", lambda e, h=hi: e.copy(h[:], wst[:]),
                   reads=["wst"], writes=[hik])
            P.step("vector", lambda e, h=hi, l=lo: e.tensor_tensor(
                l[:], wst[:], h[:], OP.subtract),
                reads=["wst", hik], writes=[lok])

        w_load(wbTd)
        w_split(wbh, wbl, "wbh", "wbl")
        w_load(wcTd)
        w_split(wch, wcl, "wch", "wcl")
        w_load(rhsd)

        for jj in range(NDT):
            bk = jj % 2
            for kk in range(NDT):
                P.step("tensor", lambda e, j=jj, k=kk, b=bk: e.transpose(
                    pstx[b][:, (k % 4) * 128:(k % 4) * 128 + 128],
                    wst[:, k * D + j * 128: k * D + (j + 1) * 128], idf[:]),
                    reads=["wst", "idf"], writes=[f"pstx{bk}"])
                if kk % 4 == 3:
                    P.step("scalar", lambda e, j=jj, k=kk, b=bk: e.copy(
                        rtst[:, j * D + (k - 3) * 128: j * D + (k + 1) * 128],
                        pstx[b][:, 0:512]),
                        reads=[f"pstx{bk}"], writes=[f"rtst{jj}"])
        for jj in range(NDT):
            P.step("vector", lambda e, j=jj: e.tensor_reduce(
                csc[:, j:j + 1], rtst[:, j * D:(j + 1) * D], AX.X, OP.max,
                apply_absolute_value=True), reads=[f"rtst{jj}"], writes=["csc"])
        P.step("vector", lambda e: e.tensor_scalar(csc[:], csc[:], 1e-12, i127[:],
                                                   OP.max, OP.mult),
               reads=["csc", "i127"], writes=["csc"])
        P.step("vector", lambda e: e.tensor_copy(icsc[:], csc[:]), reads=["csc"], writes=["icsc"])
        P.step("vector", lambda e: e.reciprocal(icsc[:], icsc[:]), reads=["icsc"], writes=["icsc"])
        for jj in range(NDT):
            sl = slice(jj * D, (jj + 1) * D)
            P.step("scalar", lambda e, s=sl, j=jj: e.activation(
                rtst[:, s], rtst[:, s], AF.Identity, bias=zcol[:], scale=icsc[:, j:j + 1]),
                reads=[f"rtst{jj}", "icsc", "zcol"], writes=[f"rtst{jj}"])
            P.step("vector", lambda e, s=sl: e.tensor_scalar(
                rtst[:, s], rtst[:, s], cxc[:], cxc[:], OP.add, OP.subtract),
                reads=[f"rtst{jj}", "cxc"], writes=[f"rtst{jj}"])
            P.step("vector", lambda e, s=sl: e.tensor_scalar(
                qrsT[:, s], rtst[:, s], 127.0, -127.0, OP.min, OP.max),
                reads=[f"rtst{jj}"], writes=["qrsT"])
        for jj in range(NDT):
            for kk in range(NDT):
                P.step("tensor", lambda e, j=jj, k=kk: e.transpose(
                    pstq[:, k * 128:(k + 1) * 128],
                    qrsT[:, k * D + j * 128: k * D + (j + 1) * 128], idb[:]),
                    reads=["qrsT", "idb"], writes=["pstq"])
            P.step("scalar", lambda e, j=jj: e.copy(qrs[:, j * D:(j + 1) * D], pstq[:]),
                   reads=["pstq"], writes=["qrs"])
        P.step("gpsimd", lambda e: e.dma_start(
            bass.AP(csrow_d, 0, [[1, 128], [128, NDT]]), csc[:],
            allow_slow_non_contiguous=True), reads=["csc"], writes=["csrowd"], dma=True)
        P.step("gpsimd", lambda e: e.dma_start(
            csbc[:], bass.AP(csrow_d, 0, [[0, 128], [1, D]])),
            reads=["csrowd"], writes=["csbc"], dma=True)

        P.alias(["x0t0", "x0t1a", "x0t1b", "x1t0", "x1t1a", "x1t1b",
                 "vh0", "vh1"], ["wst"])
        P.alias(["A0", "A1", "B0", "B1"], [f"rtst{j}" for j in range(NDT)])
        P.alias(["q0", "q1", "qT0", "qT1"], ["qrsT"])

        def sc_of(p):
            stt = stats[p]
            def sc(nm, i0=0, w=None):
                o, wd = SC[nm]
                w = wd if w is None else w
                return stt[:, o + i0: o + i0 + w]
            return sc

        def x_load(c, i=None):
            p = c & 1
            tiles = range(NTT) if i is None else [i]
            for ii_ in tiles:
                if ii_ == 0:
                    P.step("sync", lambda e, t=c * CT, pp=p: e.dma_start(
                        x_[pp][:, 0:D],
                        bass.AP(xd, t * D, [[D, 128], [1, D]])),
                        writes=[f"x{p}t0"], dma=True)
                else:
                    for sn, d0 in (("a", 0), ("b", 512)):
                        P.step("sync", lambda e, t=c * CT + 128, pp=p, dd=d0:
                               e.dma_start(
                            x_[pp][:, D + dd:D + dd + 512],
                            bass.AP(xd, t * D + dd, [[D, 128], [1, 512]])),
                            writes=[f"x{p}t1{sn}"], dma=True)

        def tx_pair(c, g):
            p = c & 1
            bk = g % 2
            xk0 = f"x{p}t0"
            xk1 = f"x{p}t1a" if g < 2 else f"x{p}t1b"
            xhk, xlk = f"xh{p}", f"xl{p}"
            items = []
            for o in range(4):
                jj, ii = 2 * g + o // 2, o % 2
                items.append((lambda e, j=jj, i=ii, oo=o, b=bk, pp=p: e.transpose(
                    pstx[b][:, oo * 128:(oo + 1) * 128],
                    x_[pp][:, i * D + j * 128: i * D + (j + 1) * 128], idf[:]),
                    [xk0 if ii == 0 else xk1, "idf"], [f"pstx{bk}"]))
            P.step_multi("tensor", items)
            P.step("scalar", lambda e, gg=g, b=bk, pp=p: e.copy(
                xh[pp][:, 2 * gg * CT: 2 * (gg + 1) * CT], pstx[b][:, 0:512]),
                reads=[f"pstx{bk}"], writes=[xhk])
            P.step("vector", lambda e, gg=g, b=bk, pp=p: e.tensor_tensor(
                xl[pp][:, 2 * gg * CT: 2 * (gg + 1) * CT], pstx[b][:, 0:512],
                xh[pp][:, 2 * gg * CT: 2 * (gg + 1) * CT], OP.subtract),
                reads=[f"pstx{bk}", xhk], writes=[xlk])

        def tq_pair(c, g):
            p = c & 1
            qk, qTk = f"q{p}", f"qT{p}"
            items = []
            for o in range(4):
                jj, ii = 2 * g + o // 2, o % 2
                items.append((lambda e, j=jj, i=ii, oo=o, pp=p: e.transpose(
                    pstq[:, oo * 128:(oo + 1) * 128],
                    q_[pp][:, i * D + j * 128: i * D + (j + 1) * 128], idb[:]),
                    [qk, "idb"], ["pstq"]))
            P.step_multi("tensor", items)
            P.step("scalar", lambda e, gg=g, pp=p: e.copy(
                qT[pp][:, 2 * gg * CT: 2 * (gg + 1) * CT], pstq[:, 0:512]),
                reads=["pstq"], writes=[qTk])

        def mm2_group(c, i, n):
            p = c & 1
            bk = (i * 2 + n) % 2
            xhk, xlk, Ak, Bk = f"xh{p}", f"xl{p}", f"A{p}", f"B{p}"
            stk = f"stats{p}"
            sc = sc_of(p)
            combos = [(xh, wch, xhk, "wch"), (xh, wcl, xhk, "wcl"),
                      (xl, wch, xlk, "wch")]
            nmm = 3 * NDT
            items = []
            idx = 0
            for (ht, wt, hk_, wk_) in combos:
                for k in range(NDT):
                    items.append((lambda e, ii=i, nn=n, kk=k, b=bk, hh=ht,
                                  ww=wt, pp=p, st=(idx == 0), sp=(idx == nmm - 1):
                                  e.matmul(
                        psm2[b][:, 0:512],
                        hh[pp][:, kk * CT + ii * 128: kk * CT + (ii + 1) * 128],
                        ww[:, kk * D + nn * 512: kk * D + (nn + 1) * 512],
                        start=st, stop=sp),
                        [hk_, wk_], [f"psm2{bk}"]))
                    idx += 1
            P.step_multi("tensor", items)

        def mm2_evac(c, i, n):
            p = c & 1
            bk = (i * 2 + n) % 2
            Ak, Bk = f"A{p}", f"B{p}"
            stk = f"stats{p}"
            sc = sc_of(p)
            P.step("vector", lambda e, ii=i, nn=n, b=bk, pp=p,
                   acc=sc("szn", i * 2 + n, 1): e.scalar_tensor_tensor(
                B_[pp][:, ii * D + nn * 512: ii * D + (nn + 1) * 512],
                psm2[b][:, 0:512], 1.0,
                A_[pp][:, ii * D + nn * 512: ii * D + (nn + 1) * 512],
                OP.mult, OP.add, accum_out=acc),
                reads=[f"psm2{bk}", Ak], writes=[Bk, stk])

        def mid_a(c):
            p = c & 1
            for i in range(NTT):
                P.step("gpsimd", lambda e, ii=i, pp=p: e.tensor_tensor(
                    A_[pp][:, ii * D:(ii + 1) * D], x_[pp][:, ii * D:(ii + 1) * D],
                    ccbc[:], OP.add),
                    reads=([f"x{p}t0"] if i == 0 else
                           [f"x{p}t1a", f"x{p}t1b"]) + ["ccbc"],
                    writes=[f"A{p}"])
            for i in range(NTT):
                P.step("gpsimd", lambda e, ii=i, pp=p: e.tensor_tensor(
                    x_[pp][:, ii * D:(ii + 1) * D], x_[pp][:, ii * D:(ii + 1) * D],
                    bbc[:], OP.add),
                    reads=([f"x{p}t0"] if i == 0 else
                           [f"x{p}t1a", f"x{p}t1b"]) + ["bbc"],
                    writes=[f"x{p}t0"] if i == 0 else
                           [f"x{p}t1a", f"x{p}t1b"])

        def mm3_group(c, i, n):
            p = c & 1
            qTk = f"qT{p}"
            P.step_multi("tensor", [
                (lambda e, ii=i, nn=n, kk=k, pp=p,
                  st=(k == 0), sp=(k == NDT - 1): e.matmul(
                    psm3[:, 0:512],
                    qT[pp][:, kk * CT + ii * 128: kk * CT + (ii + 1) * 128],
                    qrs[:, kk * D + nn * 512: kk * D + (nn + 1) * 512],
                    start=st, stop=sp),
                 [qTk, "qrs"], ["psm3"])
                for k in range(NDT)])
            sc = sc_of(p)
            P.step("scalar", lambda e, ii=i, nn=n, s=sc("sro", i, 1):
                   e.activation(
                Ab[:, ii * D + nn * 512: ii * D + (nn + 1) * 512],
                psm3[:, 0:512], AF.Identity, bias=zcol[:], scale=s),
                reads=["psm3", f"stats{p}", "zcol"], writes=["Ab"])

        def back_csr(c, i):
            p = c & 1
            sc = sc_of(p)
            P.step("gpsimd", lambda e, ii=i: e.tensor_tensor(
                Ab[:, ii * D:(ii + 1) * D], Ab[:, ii * D:(ii + 1) * D],
                csbc[:], OP.mult), reads=["Ab", "csbc"], writes=["Ab"])
            P.step("vector", lambda e, ii=i, pp=p, acc=sc("sr2", i, 1):
                   e.scalar_tensor_tensor(
                Ab[:, ii * D:(ii + 1) * D], Ab[:, ii * D:(ii + 1) * D], 1.0,
                B_[pp][:, ii * D:(ii + 1) * D], OP.mult, OP.add, accum_out=acc),
                reads=["Ab", f"B{p}"], writes=["Ab", f"stats{p}"])

        def ln_stats_steps(p, sumk, sqk, vak, rsk, nmk, i=None):
            stk = f"stats{p}"
            sc = sc_of(p)
            if i is None:
                i0, w = 0, None
            else:
                i0, w = i, 1
            P.step("vector", lambda e, o=sc(vak, i0, w), a=sc(sumk, i0, w): e.tensor_tensor(
                o, a, a, OP.mult), reads=[stk], writes=[stk])
            P.step("vector", lambda e, o=sc(vak, i0, w), b=sc(sqk, i0, w): e.scalar_tensor_tensor(
                o, o, -1.0 / D, b, OP.mult, OP.add), reads=[stk], writes=[stk])
            P.step("scalar", lambda e, o=sc(vak, i0, w): e.activation(
                o, o, AF.Ln, bias=epscol[:], scale=1.0 / D),
                reads=[stk, "epscol"], writes=[stk])
            P.step("scalar", lambda e, o=sc(rsk, i0, w), a=sc(vak, i0, w): e.activation(
                o, a, AF.Exp, bias=0.0, scale=-0.5), reads=[stk], writes=[stk])
            P.step("vector", lambda e, o=sc(nmk, i0, w), a=sc(sumk, i0, w), b=sc(rsk, i0, w):
                   e.scalar_tensor_tensor(o, a, -1.0 / D, b, OP.mult, OP.mult),
                   reads=[stk], writes=[stk])

        def ln3a_i(c, i):
            p = c & 1
            stk = f"stats{p}"
            sc = sc_of(p)
            P.step("scalar", lambda e, ii=i, pp=p, acc=sc("sr2q", i, 1): e.activation(
                B_[pp][:, ii * D:(ii + 1) * D], Ab[:, ii * D:(ii + 1) * D],
                AF.Square, accum_out=acc),
                reads=["Ab"], writes=[f"B{p}", stk])
            ln_stats_steps(p, "sr2", "sr2q", "va3", "rs3", "nm3", i)

        def ln3b_i(c, i):
            p = c & 1
            Bk = f"B{p}"
            stk = f"stats{p}"
            sc = sc_of(p)
            P.step("scalar", lambda e, ii=i, pp=p, b=sc("nm3", i, 1),
                   s=sc("rs3", i, 1): e.activation(
                B_[pp][:, ii * D:(ii + 1) * D], Ab[:, ii * D:(ii + 1) * D],
                AF.Identity, bias=b, scale=s),
                reads=["Ab", stk], writes=[Bk])

        def y_store(c):
            p = c & 1
            t0 = c * CT
            P.step("gpsimd", lambda e, t=t0, pp=p: e.dma_start(
                bass.AP(yd, t * D, [[D, 128], [128 * D, NTT], [1, D]]), B_[pp][:]),
                reads=[f"B{p}"], dma=True)

        def mm1_group(c, g):
            p = c & 1
            bk = g % 2
            xhk, xlk = f"xh{p}", f"xl{p}"
            combos = [(wbh, xh, "wbh", xhk), (wbl, xh, "wbl", xhk),
                      (wbh, xl, "wbh", xlk)]
            nmm = 3 * NDT * 2
            items = []
            idx = 0
            for (wt, xt, wk_, xk_) in combos:
                for k in range(NDT):
                    for jo in range(2):
                        jj_ = 2 * g + jo
                        items.append((lambda e, jj=jj_, kk=k, b=bk, w=wt, xx=xt,
                                      pp=p, oo=jo,
                                      st=(idx == 0), sp=(idx == nmm - 1): e.matmul(
                            psm1[b][:, oo * CT:(oo + 1) * CT],
                            w[:, kk * D + jj * 128: kk * D + (jj + 1) * 128],
                            xx[pp][:, kk * CT:(kk + 1) * CT],
                            start=st, stop=sp),
                            [wk_, xk_], [f"psm1{bk}"]))
                        idx += 1
            P.step_multi("tensor", items)

        def v_evac(c, g):
            p = c & 1
            bk = g % 2
            for jo in range(2):
                jj_ = 2 * g + jo
                P.step("scalar", lambda e, jj=jj_, b=bk, pp=p, oo=jo: e.activation(
                    vh[pp][:, jj * CT:(jj + 1) * CT],
                    psm1[b][:, oo * CT:(oo + 1) * CT], AF.Identity,
                    bias=omlbb[:, jj:jj + 1], scale=omlc[:, jj:jj + 1]),
                    reads=[f"psm1{bk}", "omlbb", "omlc"], writes=[f"vh{p}"])

        def scan_j(c, j):
            p = c & 1
            vhk = f"vh{p}"
            if c == 0:
                init = 0.0
                rd = [vhk, "lamc"]
            else:
                init = vh[1 - p][:, j * CT + CT - 1: j * CT + CT]
                rd = [vhk, "lamc", f"vh{1 - p}"]
            P.step("vector", lambda e, jj=j, pp=p, ini=init: e.tensor_tensor_scan(
                vh[pp][:, jj * CT:(jj + 1) * CT],
                lamc[:, jj:jj + 1].broadcast_to([128, CT]),
                vh[pp][:, jj * CT:(jj + 1) * CT],
                ini, OP.mult, OP.add),
                reads=rd, writes=[vhk])

        def h_split(c):
            p = c & 1
            vhk = f"vh{p}"
            xhk, xlk = f"xh{p}", f"xl{p}"
            P.step("scalar", lambda e, pp=p: e.copy(xh[pp][:], vh[pp][:]),
                   reads=[vhk], writes=[xhk])
            P.step("gpsimd", lambda e, pp=p: e.tensor_tensor(
                xl[pp][:], vh[pp][:], xh[pp][:], OP.subtract),
                reads=[vhk, xhk], writes=[xlk])

        def p1a_i(c, i):
            p = c & 1
            Ak, Bk = f"A{p}", f"B{p}"
            stk = f"stats{p}"
            stt = stats[p]
            sc = sc_of(p)
            P.step("vector", lambda e, o=sc("sz", i, 1),
                   a=sc("szn", 2 * i, 1), b=sc("szn", 2 * i + 1, 1):
                   e.tensor_tensor(o, a, b, OP.add),
                   reads=[stk], writes=[stk])
            P.step("scalar", lambda e, ii=i, pp=p, acc=sc("szq", i, 1): e.activation(
                A_[pp][:, ii * D:(ii + 1) * D], B_[pp][:, ii * D:(ii + 1) * D],
                AF.Square, accum_out=acc),
                reads=[Bk], writes=[Ak, stk])
            ln_stats_steps(p, "sz", "szq", "va1", "rs1", "nm1", i)

        def p1b_i(c, i):
            p = c & 1
            xk, Ak, Bk = (f"x{p}t0" if i == 0 else f"x{p}t1a"), f"A{p}", f"B{p}"
            xk2 = None if i == 0 else f"x{p}t1b"
            stk = f"stats{p}"
            sc = sc_of(p)
            P.step("scalar", lambda e, ii=i, pp=p, b=sc("nm1", i, 1),
                   s=sc("rs1", i, 1): e.activation(
                A_[pp][:, ii * D:(ii + 1) * D], B_[pp][:, ii * D:(ii + 1) * D],
                AF.Identity, bias=b, scale=s),
                reads=[Bk, stk], writes=[Ak])
            P.step("vector", lambda e, ii=i, pp=p: e.tensor_tensor(
                A_[pp][:, ii * D:(ii + 1) * D], A_[pp][:, ii * D:(ii + 1) * D],
                gbc[:], OP.mult), reads=[Ak, "gbc"], writes=[Ak])
            P.step("vector", lambda e, ii=i, pp=p, acc=sc("sr1", i, 1):
                   e.scalar_tensor_tensor(
                A_[pp][:, ii * D:(ii + 1) * D], A_[pp][:, ii * D:(ii + 1) * D], 1.0,
                x_[pp][:, ii * D:(ii + 1) * D], OP.mult, OP.add, accum_out=acc),
                reads=[Ak, xk] + ([xk2] if xk2 else []), writes=[Ak, stk])
        def p2_i(c, i):
            p = c & 1
            Ak, Bk, qk = f"A{p}", f"B{p}", f"q{p}"
            stk = f"stats{p}"
            sc = sc_of(p)
            P.step("scalar", lambda e, ii=i, pp=p, acc=sc("srq", i, 1): e.activation(
                B_[pp][:, ii * D:(ii + 1) * D], A_[pp][:, ii * D:(ii + 1) * D],
                AF.Square, accum_out=acc),
                reads=[Ak], writes=[Bk, stk])
            ln_stats_steps(p, "sr1", "srq", "va2", "rs2", "nm2", i)
            P.step("scalar", lambda e, ii=i, pp=p, b=sc("nm2", i, 1),
                   s=sc("rs2", i, 1): e.activation(
                B_[pp][:, ii * D:(ii + 1) * D], A_[pp][:, ii * D:(ii + 1) * D],
                AF.Identity, bias=b, scale=s),
                reads=[Ak, stk], writes=[Bk])
            P.step("vector", lambda e, ii=i, pp=p, o=sc("am", i, 1): e.tensor_reduce(
                o, B_[pp][:, ii * D:(ii + 1) * D], AX.X, OP.max,
                apply_absolute_value=True), reads=[Bk], writes=[stk])
            P.step("vector", lambda e, o=sc("sro", i, 1), a=sc("am", i, 1): e.tensor_scalar(
                o, a, 1e-12, i127[:], OP.max, OP.mult),
                reads=[stk, "i127"], writes=[stk])
            P.step("vector", lambda e, o=sc("ivs", i, 1), a=sc("sro", i, 1): e.reciprocal(o, a),
                   reads=[stk], writes=[stk])
            sl = slice(i * D, (i + 1) * D)
            P.step("scalar", lambda e, s=sl, pp=p, sv=sc("ivs", i, 1): e.activation(
                A_[pp][:, s], B_[pp][:, s], AF.Identity, bias=zcol[:], scale=sv),
                reads=[Bk, stk, "zcol"], writes=[Ak])
            P.step("vector", lambda e, s=sl, pp=p: e.tensor_scalar(
                q_[pp][:, s], A_[pp][:, s], cxc[:], cxc[:], OP.add, OP.subtract),
                reads=[Ak, "cxc"], writes=[qk])

        for k in range(NCH + 2):
            if k == 0:
                x_load(0)
                x_load(1)
            if 1 <= k <= NCH:
                mid_a(k - 1)
            if k < NCH:
                for g in range(4):
                    tx_pair(k, g)
                    if 2 <= k:
                        tq_pair(k - 2, g)
            elif 2 <= k:
                for g in range(4):
                    tq_pair(k - 2, g)
            m2 = [(i, n) for i in range(NTT) for n in range(2)] if 1 <= k <= NCH else []
            m3 = [(i, n) for i in range(NTT) for n in range(2)] if 2 <= k else []
            for gi in range(max(len(m2), len(m3))):
                if gi < len(m2):
                    mm2_group(k - 1, *m2[gi])
                    mm2_evac(k - 1, *m2[gi])
                if gi < len(m3):
                    mm3_group(k - 2, *m3[gi])
                    if m3[gi][1] == 1:
                        back_csr(k - 2, m3[gi][0])
                if gi == 1 and 1 <= k <= NCH:
                    p1a_i(k - 1, 0)
                if gi == 3 and 1 <= k <= NCH:
                    p1a_i(k - 1, 1)
            def mm1_pair(g):
                if k < NCH:
                    mm1_group(k, g)
                    v_evac(k, g)
                    if g >= 1:
                        scan_j(k, 2 * g - 2)
                        scan_j(k, 2 * g - 1)
            mm1_pair(0)
            if 1 <= k <= NCH:
                p1b_i(k - 1, 0)
            if 1 <= k and 2 <= k + 1 < NCH:
                x_load(k + 1, 0)
            if 2 <= k:
                ln3a_i(k - 2, 0)
            mm1_pair(1)
            if 1 <= k <= NCH:
                p2_i(k - 1, 0)
            if 2 <= k:
                ln3a_i(k - 2, 1)
            mm1_pair(2)
            if 1 <= k <= NCH:
                p1b_i(k - 1, 1)
            if 1 <= k and 2 <= k + 1 < NCH:
                x_load(k + 1, 1)
            if 2 <= k:
                ln3b_i(k - 2, 0)
                ln3b_i(k - 2, 1)
            mm1_pair(3)
            if 1 <= k <= NCH:
                p2_i(k - 1, 1)
            if 2 <= k:
                y_store(k - 2)
            if k < NCH:
                scan_j(k, 6)
                scan_j(k, 7)
                h_split(k)

        n_out_dma = P.dma_idx["gpsimd"]
        with nc.Block() as blk:
            @blk.sync
            def _(e):
                P.replay("sync", e, sems)

            @blk.gpsimd
            def _(e):
                P.replay("gpsimd", e, sems)
                for r in range(Planner.NDMA):
                    uses = (n_out_dma - 1 - r) // Planner.NDMA + 1 if r < n_out_dma else 0
                    if uses > 0:
                        e.wait_ge(sems[f"gpsimd_dma{r}"], 16 * uses)

            @blk.tensor
            def _(e):
                P.replay("tensor", e, sems)

            @blk.vector
            def _(e):
                P.replay("vector", e, sems)

            @blk.scalar
            def _(e):
                P.replay("scalar", e, sems)

    return nc


def kernel(L0_attn_x, L0_attn_log_lambda_raw, L0_attn_Wb, L0_attn_bb,
           L0_attn_Wc, L0_attn_bc, L0_attn_gamma, L0_attn_beta, L0_ffn_rhs):
    B, T, _ = L0_attn_x.shape
    nc = build_nc(T=T)
    nc.finalize()
    base = {
        "wbT": np.ascontiguousarray(np.asarray(L0_attn_Wb).T),
        "wcT": np.ascontiguousarray(np.asarray(L0_attn_Wc).T),
        "rhs": np.ascontiguousarray(np.asarray(L0_ffn_rhs)),
        "lraw": np.ascontiguousarray(np.asarray(L0_attn_log_lambda_raw)),
        "bb": np.ascontiguousarray(np.asarray(L0_attn_bb)),
        "bc": np.ascontiguousarray(np.asarray(L0_attn_bc)),
        "gamma": np.ascontiguousarray(np.asarray(L0_attn_gamma)),
        "beta": np.ascontiguousarray(np.asarray(L0_attn_beta)),
    }
    in_maps = [dict(base, x=np.ascontiguousarray(np.asarray(L0_attn_x)[b]))
               for b in range(B)]
    res = run_bass_kernel_spmd(nc, in_maps, core_ids=list(range(B)))
    return np.stack([np.asarray(res.results[b]["y"]) for b in range(B)], axis=0)
